# version 01
# baseline (speedup 1.0000x reference)
import contextlib
import numpy as np
import ml_dtypes
import concourse.bass as bass
import concourse.mybir as mybir
from concourse.alu_op_type import AluOpType as OP
from concourse.bass_utils import run_bass_kernel_spmd

F32 = mybir.dt.float32
BF16 = mybir.dt.bfloat16
AF = mybir.ActivationFunctionType
AXX = mybir.AxisListType.X

B, N, D = 4, 4096, 1024
H, KVH, DH = 16, 4, 64
CORES = 8
JS = N // CORES
T = B * JS
DC = D // 128
ATT_C = 20.0
LN_EPS = 1e-5

CONSTS = [-ATT_C, 1.0 / D, LN_EPS, -1.0, 0.5, 1e-5, 0.125, 1.0]
IC_NEGC, IC_INVD, IC_EPS, IC_NEG1, IC_HALF, IC_CLIP, IC_SCALE, IC_ONE = range(8)


def head_at(qc, half):
    return (qc % 4) + 8 * (qc // 4) + 4 * half


class Sem:
    def __init__(self, handle):
        self.h = handle
        self.n = 0

    def bump(self, k=1):
        self.n += k
        return self.n


class Prog:
    ENGINES = ("sync", "tensor", "vector", "scalar", "gpsimd")

    def __init__(self, nc):
        self.nc = nc
        self.steps = {e: [] for e in self.ENGINES}
        self.last_wait = {}

    def wait(self, eng, sem, val=None):
        v = sem.n if val is None else val
        if v <= 0:
            return
        key = (eng, id(sem))
        if self.last_wait.get(key, -1) >= v:
            return
        self.last_wait[key] = v
        self.steps[eng].append(lambda e, s=sem.h, vv=v: e.wait_ge(s, vv))

    def dma(self, eng, out, in_, sem):
        self.steps[eng].append(
            lambda e, o=out, i=in_, s=sem.h: e.dma_start(out=o, in_=i).then_inc(s, 16))
        return sem.bump(16)

    def op(self, eng, fn, sem=None, k=1):
        if sem is None:
            self.steps[eng].append(lambda e, f=fn: f(e))
            return None
        self.steps[eng].append(lambda e, f=fn, s=sem.h, kk=k: f(e).then_inc(s, kk))
        return sem.bump(k)

    def install(self, block):
        for eng in self.ENGINES:
            steps = self.steps[eng]

            def runner(e, _steps=steps):
                for f in _steps:
                    f(e)
            getattr(block, eng)(runner)


def build_phase_a(dbg=()):
    nc = bass.Bass("TRN2", target_bir_lowering=False)
    ap = lambda *a, **k: nc.dram_tensor(*a, **k).ap()
    x_fm = ap("x_fm", [B, D, N], F32, kind="ExternalInput")
    x_own = ap("x_own", [B, D, JS], F32, kind="ExternalInput")
    wq = ap("wq", [D, D], F32, kind="ExternalInput")
    wkv = ap("wkv", [D, 512], F32, kind="ExternalInput")
    wo = ap("wo", [D, D], F32, kind="ExternalInput")
    bo_t = ap("bo_t", [128, DC], F32, kind="ExternalInput")
    cosk = ap("cosk", [128, N], F32, kind="ExternalInput")
    sinksw = ap("sinksw", [128, N], F32, kind="ExternalInput")
    cosq = ap("cosq", [128, JS], F32, kind="ExternalInput")
    sinqsw = ap("sinqsw", [128, JS], F32, kind="ExternalInput")
    consts = ap("consts", [128, len(CONSTS)], F32, kind="ExternalInput")
    onesr = ap("onesr", [1, 128], F32, kind="ExternalInput")
    onesc = ap("onesc", [128, 1], F32, kind="ExternalInput")
    resid1 = ap("resid1", [B, D, JS], F32, kind="ExternalOutput")
    dbgd = dict(dbg)
    dbg_out = {name: ap("dbg_" + name, shape, F32, kind="ExternalOutput")
               for name, shape in dbgd.items()}

    with contextlib.ExitStack() as es:
        E = es.enter_context
        sb = lambda name, shape, dt: E(nc.sbuf_tensor(name, shape, dt))
        ps = lambda name, shape: E(nc.psum_tensor(name, shape, F32))
        qbf = sb("qbf", [128, DC * T], BF16)
        kbf = sb("kbf", [128, 2 * N], BF16)
        v65 = sb("v65", [128, 32 * 260], BF16)
        xsb = sb("xsb", [128, DC * 1024], BF16)
        wqb = sb("wqb", [128, DC * 1024], BF16)
        wkvb = sb("wkvb", [128, DC * 512], BF16)
        wob = sb("wob", [128, DC * 1024], BF16)
        sconst = sb("sconst", [128, len(CONSTS)], F32)
        sbo = sb("sbo", [128, DC], F32)
        sonesr = sb("sonesr", [1, 128], F32)
        sonesc = sb("sonesc", [128, 1], F32)
        scosq = sb("scosq", [128, JS], F32)
        ssinq = sb("ssinq", [128, JS], F32)
        ktcos = sb("ktcos", [128, 1024], F32)
        ktsin = sb("ktsin", [128, 1024], F32)
        rt1 = sb("rt1", [128, 512], F32)
        rt2 = sb("rt2", [128, 512], F32)
        expt = [sb(f"expt{i}", [128, 1024], BF16) for i in range(2)]
        attno = sb("attno", [128, DC * JS], BF16)
        srec = sb("srec", [1, JS], F32)
        sbcast = sb("sbcast", [64, JS], F32)
        sout2 = [sb(f"sout2_{i}", [128, JS], F32) for i in range(2)]
        rb = sb("rb", [128, DC * JS], F32)
        xr = sb("xr", [128, DC * JS], F32)
        ssq = sb("ssq", [128, JS], F32)
        srow = sb("srow", [1, 6 * JS], F32)
        psA = ps("psA", [128, 1024])
        psB = ps("psB", [128, 1024])
        psC = ps("psC", [128, 512])
        psD = ps("psD", [128, 512])
        psE = ps("psE", [128, 512])
        psF = ps("psF", [128, 512])

        p = Prog(nc)
        LD = Sem(E(nc.semaphore("LD")))
        LG = Sem(E(nc.semaphore("LG")))
        MM = Sem(E(nc.semaphore("MM")))
        VE = Sem(E(nc.semaphore("VE")))
        AC = Sem(E(nc.semaphore("AC")))
        ST = Sem(E(nc.semaphore("ST")))
        block = E(nc.Block())

        qbf3 = qbf[:].rearrange("p (qc t) -> p qc t", t=T)
        kbf3 = kbf[:].rearrange("p (kc n) -> p kc n", n=N)
        v653 = v65[:].rearrange("p (i c) -> p i c", c=260)
        xsb3 = xsb[:].rearrange("p (dc t) -> p dc t", t=1024)
        wqb3 = wqb[:].rearrange("p (dc f) -> p dc f", f=1024)
        wkvb3 = wkvb[:].rearrange("p (dc f) -> p dc f", f=512)
        wob3 = wob[:].rearrange("p (pc f) -> p pc f", f=1024)
        attno3 = attno[:].rearrange("p (pc t) -> p pc t", t=JS)
        rb3 = rb[:].rearrange("p (dc t) -> p dc t", t=JS)
        xr3 = xr[:].rearrange("p (dc t) -> p dc t", t=JS)
        x_fm4 = x_fm.rearrange("b (dc p) n -> p b dc n", p=128)
        x_own4 = x_own.rearrange("b (dc p) j -> p b dc j", p=128)
        resid4 = resid1.rearrange("b (dc p) j -> p b dc j", p=128)

        p.dma("sync", sconst[:], consts, LD)
        p.dma("sync", sbo[:], bo_t, LD)
        p.dma("sync", sonesr[:], onesr, LD)
        p.dma("sync", sonesc[:], onesc, LD)
        p.dma("sync", scosq[:], cosq, LD)
        p.dma("sync", ssinq[:], sinqsw, LD)
        setup_ld = LD.n
        p.dma("gpsimd", wqb3, wq.rearrange("(dc p) f -> p dc f", p=128), LG)
        p.dma("gpsimd", wkvb3, wkv.rearrange("(dc p) f -> p dc f", p=128), LG)
        p.dma("gpsimd", wob3, wo.rearrange("(pc p) f -> p pc f", p=128), LG)
        setup_lg = LG.n
        p.op("vector", lambda e: e.memset(v65[:], 1.0), sem=VE)

        def rope_emit(src_ps, cos_ap, sin_ap, out_ap):
            p.op("vector", lambda e: e.tensor_tensor(rt1[:], src_ps, cos_ap, op=OP.mult))
            for base in (0, 32, 64, 96):
                ob = base ^ 32
                p.op("vector", lambda e, b_=base, ob_=ob: e.tensor_tensor(
                    rt2[ob_:ob_ + 32, :], src_ps[b_:b_ + 32, :],
                    sin_ap[b_:b_ + 32, :], op=OP.mult))
            p.op("vector", lambda e: e.tensor_tensor(out_ap, rt1[:], rt2[:], op=OP.add),
                 sem=VE)

        pq = [psC, psD]
        pq_free = [0, 0]
        for th in range(2):
            p.wait("gpsimd", MM)
            for bb in range(2):
                xs_tok = p.dma("gpsimd",
                               xsb[:].rearrange("p (dc b j) -> p dc b j", dc=DC, b=2)[:, :, bb, :],
                               x_own4[:, 2 * th + bb, :, :], LG)
            for qc in range(DC):
                for tg in range(2):
                    bi = (qc * 2 + tg) % 2
                    buf = pq[bi]
                    p.wait("tensor", LG, xs_tok)
                    p.wait("tensor", VE, pq_free[bi])
                    for dc in range(DC):
                        p.op("tensor", lambda e, dc=dc, qc=qc, tg=tg, buf=buf: e.matmul(
                            buf[:], wqb3[:, dc, qc * 128:(qc + 1) * 128],
                            xsb3[:, dc, tg * 512:(tg + 1) * 512],
                            start=(dc == 0), stop=(dc == DC - 1)),
                            sem=MM if dc == DC - 1 else None)
                    p.wait("vector", MM)
                    p.wait("vector", LD, setup_ld)
                    tcol = (2 * th + tg) * JS
                    rope_emit(buf[:], scosq[:], ssinq[:], qbf3[:, qc, tcol:tcol + JS])
                    pq_free[bi] = VE.n

        for b in range(B):
            p.wait("tensor", AC)
            for qr in range(4):
                p.wait("gpsimd", MM)
                xk_tok = p.dma("gpsimd", xsb3,
                               x_fm4[:, b, :, qr * 1024:(qr + 1) * 1024], LG)
                p.wait("sync", VE)
                p.dma("sync", ktcos[:], cosk[:, qr * 1024:(qr + 1) * 1024], LD)
                kt_tok = p.dma("sync", ktsin[:], sinksw[:, qr * 1024:(qr + 1) * 1024], LD)
                for kc in range(2):
                    for g2 in range(2):
                        bi = (kc * 2 + g2) % 2
                        buf = pq[bi]
                        p.wait("tensor", LG, xk_tok)
                        p.wait("tensor", LD, kt_tok)
                        p.wait("tensor", VE, pq_free[bi])
                        for dc in range(DC):
                            p.op("tensor", lambda e, dc=dc, kc=kc, g2=g2, buf=buf: e.matmul(
                                buf[:], wkvb3[:, dc, kc * 128:(kc + 1) * 128],
                                xsb3[:, dc, g2 * 512:(g2 + 1) * 512],
                                start=(dc == 0), stop=(dc == DC - 1)),
                                sem=MM if dc == DC - 1 else None)
                        p.wait("vector", MM)
                        ncol = qr * 1024 + g2 * 512
                        rope_emit(buf[:], ktcos[:, g2 * 512:(g2 + 1) * 512],
                                  ktsin[:, g2 * 512:(g2 + 1) * 512],
                                  kbf3[:, kc, ncol:ncol + 512])
                        pq_free[bi] = VE.n
                for vt in range(8):
                    bi = vt % 2
                    buf = pq[bi]
                    p.wait("tensor", VE, pq_free[bi])
                    for dc in range(DC):
                        p.op("tensor", lambda e, dc=dc, vt=vt, buf=buf: e.matmul(
                            buf[:, 0:256], xsb3[:, dc, vt * 128:(vt + 1) * 128],
                            wkvb3[:, dc, 256:512],
                            start=(dc == 0), stop=(dc == DC - 1)),
                            sem=MM if dc == DC - 1 else None)
                    p.wait("vector", MM)
                    ti = qr * 8 + vt
                    p.op("vector", lambda e, ti=ti, buf=buf: e.tensor_copy(
                        v653[:, ti, 0:260].rearrange("p (g c) -> p g c", c=65)[:, :, 0:64],
                        buf[:, 0:256].rearrange("p (g c) -> p g c", c=64)), sem=VE)
                    pq_free[bi] = VE.n

            p.wait("tensor", VE)
            for h in range(H):
                kvh = h // 4
                qc = (h % 4) + 4 * (h // 8)
                half = (h // 4) % 2
                base = half * 64
                ktile = kbf3[base:base + 64, kvh // 2, :]
                qh = qbf3[base:base + 64, qc, b * JS:(b + 1) * JS]
                pvbuf = [psE, psF][h % 2]
                mm1_done = [0] * 16
                exp_done = [0] * 16
                pv_done = [0] * 16

                def emit_mm1(cg):
                    mmbuf = [psA, psB][cg % 2]
                    if cg >= 2:
                        p.wait("tensor", AC, exp_done[cg - 2])
                    for cc in range(2):
                        ck = cg * 2 + cc
                        p.op("tensor", lambda e, ck=ck, cc=cc, mmbuf=mmbuf, ktile=ktile, qh=qh:
                             e.matmul(mmbuf[:, cc * 512:(cc + 1) * 512],
                                      ktile[:, ck * 128:(ck + 1) * 128], qh,
                                      start=True, stop=True),
                             sem=MM if cc == 1 else None)
                    mm1_done[cg] = MM.n

                def emit_exp(cg):
                    mmbuf = [psA, psB][cg % 2]
                    p.wait("scalar", MM, mm1_done[cg])
                    if cg >= 2:
                        p.wait("scalar", MM, pv_done[cg - 2])
                    p.op("scalar", lambda e, cg=cg, mmbuf=mmbuf: e.activation(
                        expt[cg % 2][:], mmbuf[:], AF.Exp,
                        bias=sconst[:, IC_NEGC:IC_NEGC + 1], scale=0.125), sem=AC)
                    exp_done[cg] = AC.n

                def emit_pv(cg):
                    p.wait("tensor", AC, exp_done[cg])
                    for cc in range(2):
                        ck = cg * 2 + cc
                        p.op("tensor", lambda e, ck=ck, cc=cc, cg=cg, kvh=kvh, pvbuf=pvbuf:
                             e.matmul(pvbuf[0:65, :], v653[:, ck, kvh * 65:kvh * 65 + 65],
                                      expt[cg % 2][:, cc * 512:(cc + 1) * 512],
                                      start=(ck == 0), stop=(ck == 31),
                                      skip_group_check=True),
                             sem=MM if cc == 1 else None)
                    pv_done[cg] = MM.n

                emit_mm1(0)
                emit_exp(0)
                for cg in range(1, 16):
                    emit_mm1(cg)
                    emit_exp(cg)
                    emit_pv(cg - 1)
                emit_pv(15)
                p.wait("vector", MM, pv_done[15])
                p.op("vector", lambda e, pvbuf=pvbuf: e.reciprocal(srec[:], pvbuf[64:65, :]), sem=VE)
                p.wait("tensor", VE)
                p.wait("tensor", AC)
                p.op("tensor", lambda e: e.matmul(psC[0:64, :], sonesr[0:1, 0:64], srec[:],
                                                  start=True, stop=True), sem=MM)
                p.wait("scalar", MM)
                p.op("scalar", lambda e: e.copy(sbcast[:], psC[0:64, :]), sem=AC)
                p.wait("vector", AC)
                p.op("vector", lambda e, pvbuf=pvbuf, qc=qc, base=base: e.tensor_tensor(
                    attno3[base:base + 64, qc, :], pvbuf[0:64, :], sbcast[:], op=OP.mult), sem=VE)
                pq_free[0] = pq_free[1] = VE.n

            p.wait("sync", VE)
            p.wait("sync", ST)
            xr_tok = p.dma("sync", xr3, x_own4[:, b, :, :], LD)
            attno_ready = VE.n
            for fc in range(DC):
                bi = fc % 2
                p.wait("tensor", VE)
                for pc in range(DC):
                    p.op("tensor", lambda e, pc=pc, fc=fc, bi=bi: e.matmul(
                        [psC, psD][bi][:], wob3[:, pc, fc * 128:(fc + 1) * 128],
                        attno3[:, pc, :],
                        start=(pc == 0), stop=(pc == DC - 1)),
                        sem=MM if pc == DC - 1 else None)
                p.wait("vector", MM)
                p.wait("vector", LD, xr_tok)
                p.wait("vector", ST)
                p.op("vector", lambda e, fc=fc, bi=bi: e.tensor_scalar(
                    sout2[bi][:], [psC, psD][bi][:], sbo[:, fc:fc + 1], None,
                    op0=OP.add), sem=VE)
                p.op("vector", lambda e, fc=fc, bi=bi: e.tensor_tensor(
                    rb3[:, fc, :], sout2[bi][:], xr3[:, fc, :], op=OP.add), sem=VE)
            sq_read = 0
            for fc in range(DC):
                p.wait("scalar", VE)
                p.wait("scalar", MM, sq_read)
                p.op("scalar", lambda e, fc=fc: e.activation(
                    ssq[:], rb3[:, fc, :], AF.Square, scale=1.0), sem=AC)
                p.wait("tensor", VE)
                p.op("tensor", lambda e, fc=fc: e.matmul(
                    psE[0:1, :], sonesc[:], rb3[:, fc, :],
                    start=(fc == 0), stop=(fc == DC - 1), skip_group_check=True),
                    sem=MM if fc == DC - 1 else None)
                p.wait("tensor", AC)
                p.op("tensor", lambda e, fc=fc: e.matmul(
                    psF[0:1, :], sonesc[:], ssq[:],
                    start=(fc == 0), stop=(fc == DC - 1), skip_group_check=True),
                    sem=MM)
                sq_read = MM.n
            ROW = lambda i: srow[0:1, i * JS:(i + 1) * JS]
            p.wait("vector", MM)
            p.op("vector", lambda e: e.tensor_scalar(ROW(0), psE[0:1, :], 1.0 / D, None, op0=OP.mult))
            p.op("vector", lambda e: e.tensor_scalar(ROW(1), psF[0:1, :], 1.0 / D, None, op0=OP.mult))
            p.op("vector", lambda e: e.tensor_tensor(ROW(2), ROW(0), ROW(0), op=OP.mult))
            p.op("vector", lambda e: e.tensor_tensor(ROW(2), ROW(1), ROW(2), op=OP.subtract), sem=VE)
            p.wait("scalar", VE)
            p.op("scalar", lambda e: e.activation(ROW(3), ROW(2), AF.Ln,
                                                  bias=sconst[0:1, IC_EPS:IC_EPS + 1], scale=1.0))
            p.op("scalar", lambda e: e.activation(ROW(4), ROW(3), AF.Exp, scale=-0.5), sem=AC)
            p.wait("vector", AC)
            p.op("vector", lambda e: e.tensor_tensor(ROW(5), ROW(0), ROW(4), op=OP.mult))
            p.op("vector", lambda e: e.tensor_scalar(ROW(5), ROW(5), -1.0, None, op0=OP.mult), sem=VE)
            p.wait("tensor", VE)
            p.op("tensor", lambda e: e.matmul(psA[:, 0:512], sonesr[:], ROW(4), start=True, stop=True))
            p.op("tensor", lambda e: e.matmul(psA[:, 512:1024], sonesr[:], ROW(5), start=True, stop=True),
                 sem=MM)
            p.wait("vector", MM)
            for fc in range(DC):
                p.op("vector", lambda e, fc=fc: e.tensor_tensor(
                    ssq[:], rb3[:, fc, :], psA[:, 0:512], op=OP.mult))
                p.op("vector", lambda e, fc=fc: e.tensor_tensor(
                    rb3[:, fc, :], ssq[:], psA[:, 512:1024], op=OP.add), sem=VE)
            p.wait("sync", VE)
            p.dma("sync", resid4[:, b, :, :], rb3, ST)

        dbg_src = {"q": qbf, "k": kbf, "v": v65, "attno": attno}
        for name in dbgd:
            p.wait("gpsimd", VE)
            p.wait("gpsimd", MM)
            p.dma("gpsimd", dbg_out[name], dbg_src[name][:], ST)
        p.wait("sync", ST)
        p.wait("gpsimd", ST)
        p.install(block)
    return nc


def build_phase_b(dbg=(), experts=None, no_ln=False):
    NE = 9
    nc = bass.Bass("TRN2", target_bir_lowering=False)
    ap = lambda *a, **k: nc.dram_tensor(*a, **k).ap()
    resid_in = ap("resid_in", [B, D, JS], F32, kind="ExternalInput")
    w1T = ap("w1T", [NE, D, 2048], F32, kind="ExternalInput")
    w1nat = ap("w1nat", [NE, 2048, D], F32, kind="ExternalInput")
    w2T = ap("w2T", [NE, 2048, D], F32, kind="ExternalInput")
    w2nat = ap("w2nat", [NE, D, 2048], F32, kind="ExternalInput")
    routerT = ap("routerT", [D, 8], F32, kind="ExternalInput")
    gamma_t = ap("gamma_t", [128, DC], F32, kind="ExternalInput")
    beta_t = ap("beta_t", [128, DC], F32, kind="ExternalInput")
    consts = ap("consts", [128, len(CONSTS)], F32, kind="ExternalInput")
    onesr = ap("onesr", [1, 128], F32, kind="ExternalInput")
    onesc = ap("onesc", [128, 1], F32, kind="ExternalInput")
    negr = ap("negr", [1, 128], F32, kind="ExternalInput")
    ident = ap("ident", [128, 128], F32, kind="ExternalInput")
    resid2 = ap("resid2", [B, D, JS], F32, kind="ExternalOutput")
    hs1s = ap("hs1s", [16, 128], F32, kind="Internal")
    hs2s = ap("hs2s", [8, 128], F32, kind="Internal")
    gscr = ap("gscr", [8, T], F32, kind="Internal")
    dbgd = dict(dbg)
    dbg_out = {name: ap("dbg_" + name, shape, F32, kind="ExternalOutput")
               for name, shape in dbgd.items()}

    with contextlib.ExitStack() as es:
        E = es.enter_context
        sb = lambda name, shape, dt: E(nc.sbuf_tensor(name, shape, dt))
        ps = lambda name, shape: E(nc.psum_tensor(name, shape, F32))
        rbig = sb("rbig", [128, DC * T], F32)
        hbf = sb("hbf", [128, DC * T], BF16)
        t1q = sb("t1q", [128, DC * 2048], BF16)
        t2q = sb("t2q", [128, 16 * D], BF16)
        h1q = sb("h1q", [128, 16 * 512], BF16)
        slab0 = sb("slab0", [128, 512], F32)
        slab = [slab0, slab0]
        sg = sb("sg", [8, T], F32)
        sc1 = [sb(f"sc1_{i}", [128, 16], F32) for i in range(2)]
        sc2 = [sb(f"sc2_{i}", [128, 8], F32) for i in range(2)]
        hs1r = sb("hs1r", [1, 2048], F32)
        hs2r = hs1r
        hscol = sb("hscol", [128, 16], F32)
        coltmp = sb("coltmp", [128, 1], F32)
        hsT = sb("hsT", [16, 128], F32)
        srw = sb("srw", [128, DC * 8], F32)
        sgamma = sb("sgamma", [128, DC], F32)
        sbeta = sb("sbeta", [128, DC], F32)
        sconst = sb("sconst", [128, len(CONSTS)], F32)
        sonesr = sb("sonesr", [1, 128], F32)
        sonesc = sb("sonesc", [128, 1], F32)
        snegr = sb("snegr", [1, 128], F32)
        sident = sb("sident", [128, 128], F32)
        spe = sb("spe", [128, 8], F32)
        snm = sb("snm", [128, 4], F32)
        smask = sb("smask", [128, 8], F32)
        ytmp = [sb(f"ytmp{i}", [128, 512], F32) for i in range(2)]
        ssq = sb("ssq", [128, 512], F32)
        qscr0 = sb("qscr0", [128, 512], F32)
        qscr = [qscr0, ssq]
        srow = hs1r
        grow = [sg[0:1, i * 512:(i + 1) * 512] for i in range(2)]
        psA = ps("psA", [128, 1024])
        psB = ps("psB", [128, 1024])
        psC = ps("psC", [128, 512])
        psD = ps("psD", [128, 512])
        psE = ps("psE", [128, 512])
        psF = ps("psF", [128, 512])

        p = Prog(nc)
        LD = Sem(E(nc.semaphore("LD")))
        LG = Sem(E(nc.semaphore("LG")))
        MM = Sem(E(nc.semaphore("MM")))
        VE = Sem(E(nc.semaphore("VE")))
        AC = Sem(E(nc.semaphore("AC")))
        GP = Sem(E(nc.semaphore("GP")))
        ST = Sem(E(nc.semaphore("ST")))
        block = E(nc.Block())

        rbig3 = rbig[:].rearrange("p (dc t) -> p dc t", t=T)
        hbf3 = hbf[:].rearrange("p (dc t) -> p dc t", t=T)
        t1q3 = t1q[:].rearrange("p (dc f) -> p dc f", f=2048)
        t2q3 = t2q[:].rearrange("p (fc f) -> p fc f", f=D)
        h1q3 = h1q[:].rearrange("p (fc t) -> p fc t", t=512)
        srw3 = srw[:].rearrange("p (dc e) -> p dc e", e=8)
        rin4 = resid_in.rearrange("b (dc p) j -> p b dc j", p=128)
        rout4 = resid2.rearrange("b (dc p) j -> p b dc j", p=128)
        ROW = lambda i: srow[0:1, i * 512:(i + 1) * 512]

        for b in range(B):
            p.dma("sync", rbig[:].rearrange("p (dc b j) -> p dc b j", dc=DC, b=B)[:, :, b, :],
                  rin4[:, b, :, :], LD)
        p.dma("sync", sconst[:], consts, LD)
        p.dma("sync", sgamma[:], gamma_t, LD)
        p.dma("sync", sbeta[:], beta_t, LD)
        p.dma("sync", sonesr[:], onesr, LD)
        p.dma("sync", sonesc[:], onesc, LD)
        p.dma("sync", sident[:], ident, LD)
        p.dma("sync", snegr[:], negr, LD)
        p.dma("sync", srw3, routerT.rearrange("(dc p) e -> p dc e", p=128), LD)
        setup_ld = LD.n

        for tg in range(B):
            tcol = tg * 512
            sq_read = 0
            p.wait("tensor", LD, setup_ld)
            for fc in range(DC):
                p.wait("scalar", LD, setup_ld)
                p.wait("scalar", VE)
                p.wait("scalar", MM, sq_read)
                p.op("scalar", lambda e, fc=fc, tcol=tcol: e.activation(
                    ssq[:], rbig3[:, fc, tcol:tcol + 512], AF.Square, scale=1.0), sem=AC)
                p.op("tensor", lambda e, fc=fc, tcol=tcol: e.matmul(
                    psE[0:1, :], sonesc[:], rbig3[:, fc, tcol:tcol + 512],
                    start=(fc == 0), stop=(fc == DC - 1), skip_group_check=True),
                    sem=MM if fc == DC - 1 else None)
                p.wait("tensor", AC)
                p.op("tensor", lambda e, fc=fc: e.matmul(
                    psF[0:1, :], sonesc[:], ssq[:],
                    start=(fc == 0), stop=(fc == DC - 1), skip_group_check=True), sem=MM)
                sq_read = MM.n
            p.wait("vector", MM)
            p.op("vector", lambda e: e.tensor_scalar(ROW(0), psE[0:1, :], 1.0 / D, None, op0=OP.mult))
            p.op("vector", lambda e: e.tensor_scalar(ROW(1), psF[0:1, :], 1.0 / D, None, op0=OP.mult))
            p.op("vector", lambda e: e.tensor_tensor(ROW(2), ROW(0), ROW(0), op=OP.mult))
            p.op("vector", lambda e: e.tensor_tensor(ROW(2), ROW(1), ROW(2), op=OP.subtract), sem=VE)
            p.wait("scalar", VE)
            p.op("scalar", lambda e: e.activation(ROW(3), ROW(2), AF.Ln,
                                                  bias=sconst[0:1, IC_EPS:IC_EPS + 1], scale=1.0))
            p.op("scalar", lambda e: e.activation(ROW(2), ROW(3), AF.Exp, scale=-0.5), sem=AC)
            p.wait("vector", AC)
            p.op("vector", lambda e: e.tensor_tensor(ROW(3), ROW(0), ROW(2), op=OP.mult))
            p.op("vector", lambda e: e.tensor_scalar(ROW(3), ROW(3), -1.0, None, op0=OP.mult), sem=VE)
            p.wait("tensor", VE)
            p.op("tensor", lambda e: e.matmul(psA[:, 0:512], sonesr[:], ROW(2), start=True, stop=True))
            p.op("tensor", lambda e: e.matmul(psA[:, 512:1024], sonesr[:], ROW(3), start=True, stop=True),
                 sem=MM)
            p.wait("vector", MM)
            for fc in range(DC):
                p.op("vector", lambda e, fc=fc, tcol=tcol: e.tensor_tensor(
                    ssq[:], rbig3[:, fc, tcol:tcol + 512], psA[:, 0:512], op=OP.mult))
                p.op("vector", lambda e, fc=fc: e.tensor_tensor(
                    ssq[:], ssq[:], psA[:, 512:1024], op=OP.add))
                p.op("vector", lambda e, fc=fc, tcol=tcol: e.tensor_scalar(
                    hbf3[:, fc, tcol:tcol + 512], ssq[:], sgamma[:, fc:fc + 1],
                    sbeta[:, fc:fc + 1], op0=OP.mult, op1=OP.add), sem=VE)
                p.op("vector", lambda e, fc=fc: e.tensor_scalar(
                    h1q3[:, fc * 2:fc * 2 + 2, :].rearrange("p a t -> p (a t)").bitcast(F32),
                    ssq[:], sgamma[:, fc:fc + 1], sbeta[:, fc:fc + 1],
                    op0=OP.mult, op1=OP.add), sem=VE)
            hf3 = h1q[:].bitcast(F32).rearrange("p (dc t) -> p dc t", t=512)
            for tc in range(4):
                p.wait("tensor", VE)
                for dc in range(DC):
                    p.op("tensor", lambda e, dc=dc, tc=tc: e.matmul(
                        psC[:, 0:8], hf3[:, dc, tc * 128:(tc + 1) * 128], srw3[:, dc, :],
                        start=(dc == 0), stop=(dc == DC - 1)),
                        sem=MM if dc == DC - 1 else None)
                p.wait("vector", MM)
                p.op("vector", lambda e: e.tensor_reduce(snm[:, 0:1], psC[:, 0:8], axis=AXX,
                                                         op=OP.max, negate=True), sem=VE)
                p.wait("scalar", VE)
                p.op("scalar", lambda e: e.activation(spe[:], psC[:, 0:8], AF.Exp,
                                                      bias=snm[:, 0:1], scale=1.0,
                                                      accum_out=snm[:, 1:2]), sem=AC)
                p.wait("vector", AC)
                p.op("vector", lambda e: e.tensor_scalar(smask[:], spe[:], 0.99999994, None,
                                                         op0=OP.is_gt), sem=VE)
                p.op("vector", lambda e: e.reciprocal(snm[:, 3:4], snm[:, 1:2]), sem=VE)
                p.wait("vector", VE)
                p.op("vector", lambda e: e.tensor_tensor(smask[:], smask[:], spe[:], op=OP.mult), sem=VE)
                p.wait("vector", VE)
                p.op("vector", lambda e: e.tensor_scalar(smask[:], smask[:], snm[:, 3:4], None,
                                                         op0=OP.mult), sem=VE)
                p.wait("tensor", VE)
                p.op("tensor", lambda e: e.transpose(psD[0:8, 0:128], smask[:], sident[:]), sem=MM)
                p.wait("vector", MM)
                p.op("vector", lambda e, tg=tg, tc=tc: e.tensor_copy(
                    sg[:, tg * 512 + tc * 128: tg * 512 + (tc + 1) * 128], psD[0:8, 0:128]), sem=VE)

        p.wait("sync", VE)
        gscr_tok = p.dma("sync", gscr, sg[:], ST)

        def emit_quant(e, which):
            par = e % 2
            if which == 0:
                nslab, flen, red_n = 16, 2048, D
                nat, wT = w1nat, w1T
                scol, hrow, tq3, scr = sc1[par], hs1r, t1q3, hs1s
                ndc = DC
            else:
                nslab, flen, red_n = 8, D, 2048
                nat, wT = w2nat, w2T
                scol, hrow, tq3, scr = sc2[par], hs2r, t2q3, hs2s
                ndc = 16
            for s in range(nslab):
                for pc_ in range(red_n // 512):
                    p.wait("gpsimd", AC)
                    p.wait("gpsimd", VE)
                    tok = p.dma("gpsimd", slab[0][:],
                                nat[e, s * 128:(s + 1) * 128, pc_ * 512:(pc_ + 1) * 512], LG)
                    p.wait("scalar", LG, tok)
                    p.op("scalar", lambda e_: e_.activation(
                        qscr[0][:], slab[0][:], AF.Abs, accum_out=coltmp[:]), sem=AC)
                    p.wait("vector", AC)
                    if pc_ == 0:
                        p.op("vector", lambda e_, s=s, scol=scol: e_.tensor_copy(
                            scol[:, s:s + 1], coltmp[:]), sem=VE)
                    else:
                        p.op("vector", lambda e_, s=s, scol=scol: e_.tensor_tensor(
                            scol[:, s:s + 1], scol[:, s:s + 1], coltmp[:], op=OP.add), sem=VE)
            p.wait("vector", VE)
            p.op("vector", lambda e_, scol=scol, nslab=nslab, red_n=red_n: e_.tensor_scalar(
                scol[:, 0:nslab], scol[:, 0:nslab], 1.0 / red_n, 1e-5, op0=OP.mult, op1=OP.max),
                 sem=VE)
            p.wait("vector", VE)
            p.op("vector", lambda e_, scol=scol, nslab=nslab: e_.tensor_scalar(
                hscol[:, 0:nslab], scol[:, 0:nslab], 0.5, None, op0=OP.mult), sem=VE)
            p.wait("tensor", VE)
            p.op("tensor", lambda e_, nslab=nslab: e_.transpose(
                psB[0:nslab, 0:128], hscol[:, 0:nslab], sident[:]), sem=MM)
            p.wait("vector", MM)
            p.op("vector", lambda e_, nslab=nslab: e_.tensor_copy(hsT[0:nslab, :], psB[0:nslab, 0:128]),
                 sem=VE)
            p.wait("sync", VE)
            p.dma("sync", scr[0:nslab, :], hsT[0:nslab, :], ST)
            p.wait("sync", ST)
            p.wait("sync", MM)
            hrow_tok = p.dma("sync", hrow[0:1, 0:flen],
                             scr.rearrange("a b -> (a b)").unsqueeze(0)[:, 0:flen], LD)
            for fg in range(flen // 512):
                p.wait("tensor", LD, hrow_tok)
                p.wait("tensor", VE)
                p.op("tensor", lambda e_, fg=fg, hrow=hrow: e_.matmul(
                    psB[:, 0:512], sonesr[:], hrow[0:1, fg * 512:(fg + 1) * 512],
                    start=True, stop=True))
                p.op("tensor", lambda e_, fg=fg, hrow=hrow: e_.matmul(
                    psB[:, 512:1024], snegr[:], hrow[0:1, fg * 512:(fg + 1) * 512],
                    start=True, stop=True), sem=MM)
                hmm_tok = MM.n
                for dc in range(ndc):
                    si = dc % 2
                    p.wait("gpsimd", VE)
                    tok = p.dma("gpsimd", slab[si][:],
                                wT[e, dc * 128:(dc + 1) * 128, fg * 512:(fg + 1) * 512], LG)
                    p.wait("vector", LG, tok)
                    p.wait("vector", MM, hmm_tok)
                    p.op("vector", lambda e_, si=si: e_.tensor_tensor(
                        qscr[0][:], slab[si][:], psB[:, 0:512], op=OP.is_gt))
                    p.op("vector", lambda e_, si=si: e_.tensor_tensor(
                        qscr[1][:], slab[si][:], psB[:, 512:1024], op=OP.is_lt))
                    p.op("vector", lambda e_, dc=dc, fg=fg, tq3=tq3: e_.tensor_tensor(
                        tq3[:, dc, fg * 512:(fg + 1) * 512], qscr[0][:], qscr[1][:],
                        op=OP.subtract), sem=VE)

        def emit_mm1(e):
            par = e % 2
            for tq in range(B):
                for fc in range(16):
                    bi = (tq * 16 + fc) % 2
                    buf = [psC, psD][bi]
                    p.wait("tensor", VE)
                    p.wait("tensor", AC)
                    for dc in range(DC):
                        p.op("tensor", lambda e_, dc=dc, fc=fc, tq=tq, buf=buf, e=e: e_.matmul(
                            buf[:], t1q3[:, dc, fc * 128:(fc + 1) * 128],
                            hbf3[:, dc, tq * 512:(tq + 1) * 512],
                            start=(dc == 0), stop=(dc == DC - 1)),
                            sem=MM if dc == DC - 1 else None)
                    p.wait("scalar", MM)
                    p.op("scalar", lambda e_, fc=fc, bi=bi, par=par: e_.activation(
                        h1q3[:, fc, :], [psC, psD][bi][:], AF.Silu,
                        scale=sc1[par][:, fc:fc + 1]), sem=AC)
                yield tq

        def emit_expert(e):
            par = e % 2
            gen = emit_mm1(e)
            for tq in gen:
                for dc8 in range(DC):
                    bi = dc8 % 2
                    buf = [psE, psF][bi]
                    p.wait("tensor", AC)
                    p.wait("tensor", VE)
                    for fc in range(16):
                        p.op("tensor", lambda e_, fc=fc, dc8=dc8, buf=buf, e=e: e_.matmul(
                            buf[:], t2q3[:, fc, dc8 * 128:(dc8 + 1) * 128], h1q3[:, fc, :],
                            start=(fc == 0), stop=(fc == 15)),
                            sem=MM if fc == 15 else None)
                    if e < 8:
                        gi = dc8 % 2
                        p.wait("sync", ST, gscr_tok)
                        p.wait("sync", MM)
                        gtok = p.dma("sync", grow[gi], gscr[e:e + 1, tq * 512:(tq + 1) * 512], LD)
                        p.wait("tensor", LD, gtok)
                        p.op("tensor", lambda e_, gi=gi: e_.matmul(
                            psA[:, 0:512], sonesr[:], grow[gi],
                            start=True, stop=True), sem=MM)
                    p.wait("vector", MM)
                    p.wait("vector", GP)
                    p.op("vector", lambda e_, bi=bi, par=par, dc8=dc8, buf=buf: e_.tensor_scalar(
                        ytmp[bi][:], buf[:], sc2[par][:, dc8:dc8 + 1], None, op0=OP.mult), sem=VE)
                    if e < 8:
                        p.op("vector", lambda e_, bi=bi: e_.tensor_tensor(
                            ytmp[bi][:], ytmp[bi][:], psA[:, 0:512], op=OP.mult), sem=VE)
                    p.wait("gpsimd", VE)
                    p.op("gpsimd", lambda e_, bi=bi, dc8=dc8, tq=tq: e_.tensor_tensor(
                        rbig3[:, dc8, tq * 512:(tq + 1) * 512],
                        rbig3[:, dc8, tq * 512:(tq + 1) * 512], ytmp[bi][:], op=OP.add), sem=GP)

        elist = list(experts) if experts is not None else list(range(NE))
        emit_quant(elist[0], 0)
        emit_quant(elist[0], 1)
        for i, e in enumerate(elist):
            emit_expert(e)
            if i + 1 < len(elist):
                emit_quant(elist[i + 1], 0)
                emit_quant(elist[i + 1], 1)

        for tg in range(B):
            if no_ln:
                p.wait("sync", GP)
                p.wait("sync", VE)
                p.dma("sync", rout4[:, tg, :, :],
                      rbig[:].rearrange("p (dc b j) -> p dc b j", dc=DC, b=B)[:, :, tg, :], ST)
                continue
            tcol = tg * 512
            sq_read = 0
            p.wait("tensor", GP)
            for fc in range(DC):
                p.wait("scalar", GP)
                p.wait("scalar", VE)
                p.wait("scalar", MM, sq_read)
                p.op("scalar", lambda e, fc=fc, tcol=tcol: e.activation(
                    ssq[:], rbig3[:, fc, tcol:tcol + 512], AF.Square, scale=1.0), sem=AC)
                p.op("tensor", lambda e, fc=fc, tcol=tcol: e.matmul(
                    psE[0:1, :], sonesc[:], rbig3[:, fc, tcol:tcol + 512],
                    start=(fc == 0), stop=(fc == DC - 1), skip_group_check=True),
                    sem=MM if fc == DC - 1 else None)
                p.wait("tensor", AC)
                p.op("tensor", lambda e, fc=fc: e.matmul(
                    psF[0:1, :], sonesc[:], ssq[:],
                    start=(fc == 0), stop=(fc == DC - 1), skip_group_check=True), sem=MM)
                sq_read = MM.n
            p.wait("vector", MM)
            p.op("vector", lambda e: e.tensor_scalar(ROW(0), psE[0:1, :], 1.0 / D, None, op0=OP.mult))
            p.op("vector", lambda e: e.tensor_scalar(ROW(1), psF[0:1, :], 1.0 / D, None, op0=OP.mult))
            p.op("vector", lambda e: e.tensor_tensor(ROW(2), ROW(0), ROW(0), op=OP.mult))
            p.op("vector", lambda e: e.tensor_tensor(ROW(2), ROW(1), ROW(2), op=OP.subtract), sem=VE)
            p.wait("scalar", VE)
            p.op("scalar", lambda e: e.activation(ROW(3), ROW(2), AF.Ln,
                                                  bias=sconst[0:1, IC_EPS:IC_EPS + 1], scale=1.0))
            p.op("scalar", lambda e: e.activation(ROW(2), ROW(3), AF.Exp, scale=-0.5), sem=AC)
            p.wait("vector", AC)
            p.op("vector", lambda e: e.tensor_tensor(ROW(3), ROW(0), ROW(2), op=OP.mult))
            p.op("vector", lambda e: e.tensor_scalar(ROW(3), ROW(3), -1.0, None, op0=OP.mult), sem=VE)
            p.wait("tensor", VE)
            p.op("tensor", lambda e: e.matmul(psA[:, 0:512], sonesr[:], ROW(2), start=True, stop=True))
            p.op("tensor", lambda e: e.matmul(psA[:, 512:1024], sonesr[:], ROW(3), start=True, stop=True),
                 sem=MM)
            p.wait("vector", MM)
            for fc in range(DC):
                p.op("vector", lambda e, fc=fc, tcol=tcol: e.tensor_tensor(
                    ssq[:], rbig3[:, fc, tcol:tcol + 512], psA[:, 0:512], op=OP.mult))
                p.op("vector", lambda e, fc=fc, tcol=tcol: e.tensor_tensor(
                    rbig3[:, fc, tcol:tcol + 512], ssq[:], psA[:, 512:1024], op=OP.add), sem=VE)
            p.wait("sync", VE)
            p.dma("sync", rout4[:, tg, :, :],
                  rbig[:].rearrange("p (dc b j) -> p dc b j", dc=DC, b=B)[:, :, tg, :], ST)

        dbg_src = {"h": hbf, "t1q": t1q, "t2q": t2q, "g": sg, "rmid": rbig,
                   "sc1": sc1[0], "sc2": sc2[0], "hsT": hsT, "h1": h1q, "srow": hs1r,
                   "spe": spe, "smask": smask, "snm": snm}
        for name in dbgd:
            p.wait("gpsimd", VE)
            p.wait("gpsimd", MM)
            p.wait("gpsimd", GP)
            if name == "gates":
                p.dma("gpsimd", dbg_out[name], gscr, ST)
            else:
                p.dma("gpsimd", dbg_out[name], dbg_src[name][:], ST)
        p.wait("sync", ST)
        p.wait("gpsimd", ST)
        p.install(block)
    return nc



def emit_ln_store(p, nc, sems, rview, psE, psF, psA, sonesc, sonesr, sconst, ssq, ROW,
                  store=None, guard="vector"):
    LD, MM, VE, AC, GP, ST = (sems[k] for k in ("LD", "MM", "VE", "AC", "GP", "ST"))
    for tg in range(B):
        tcol = tg * 512
        sq_read = 0
        p.wait("tensor", GP)
        p.wait("tensor", VE)
        for fc in range(DC):
            p.wait("scalar", GP)
            p.wait("scalar", VE)
            p.wait("scalar", MM, sq_read)
            p.op("scalar", lambda e, fc=fc, tcol=tcol: e.activation(
                ssq[:], rview[:, fc, tcol:tcol + 512], AF.Square, scale=1.0), sem=AC)
            p.op("tensor", lambda e, fc=fc, tcol=tcol: e.matmul(
                psE[0:1, :], sonesc[:], rview[:, fc, tcol:tcol + 512],
                start=(fc == 0), stop=(fc == DC - 1), skip_group_check=True),
                sem=MM if fc == DC - 1 else None)
            p.wait("tensor", AC)
            p.op("tensor", lambda e, fc=fc: e.matmul(
                psF[0:1, :], sonesc[:], ssq[:],
                start=(fc == 0), stop=(fc == DC - 1), skip_group_check=True), sem=MM)
            sq_read = MM.n
        p.wait("vector", MM)
        p.op("vector", lambda e: e.tensor_scalar(ROW(0), psE[0:1, :], 1.0 / D, None, op0=OP.mult))
        p.op("vector", lambda e: e.tensor_scalar(ROW(1), psF[0:1, :], 1.0 / D, None, op0=OP.mult))
        p.op("vector", lambda e: e.tensor_tensor(ROW(2), ROW(0), ROW(0), op=OP.mult))
        p.op("vector", lambda e: e.tensor_tensor(ROW(2), ROW(1), ROW(2), op=OP.subtract), sem=VE)
        p.wait("scalar", VE)
        p.op("scalar", lambda e: e.activation(ROW(3), ROW(2), AF.Ln,
                                              bias=sconst[0:1, IC_EPS:IC_EPS + 1], scale=1.0))
        p.op("scalar", lambda e: e.activation(ROW(2), ROW(3), AF.Exp, scale=-0.5), sem=AC)
        p.wait("vector", AC)
        p.op("vector", lambda e: e.tensor_tensor(ROW(3), ROW(0), ROW(2), op=OP.mult), sem=VE)
        p.wait("vector", VE)
        p.op("vector", lambda e: e.tensor_scalar(ROW(3), ROW(3), -1.0, None, op0=OP.mult), sem=VE)
        p.wait("tensor", VE)
        p.op("tensor", lambda e: e.matmul(psA[:, 0:512], sonesr[:], ROW(2), start=True, stop=True))
        p.op("tensor", lambda e: e.matmul(psA[:, 512:1024], sonesr[:], ROW(3), start=True, stop=True),
             sem=MM)
        p.wait("vector", MM)
        for fc in range(DC):
            p.op("vector", lambda e, fc=fc, tcol=tcol: e.tensor_tensor(
                ssq[:], rview[:, fc, tcol:tcol + 512], psA[:, 0:512], op=OP.mult))
            p.op("vector", lambda e, fc=fc, tcol=tcol: e.tensor_tensor(
                rview[:, fc, tcol:tcol + 512], ssq[:], psA[:, 512:1024], op=OP.add), sem=VE)
        if store is not None:
            p.wait("sync", VE)
            dst, srcap = store(tg)
            p.dma("sync", dst, srcap, ST)


def emit_router(p, nc, sems, src_fm, srw3, psC, psD, spe, snm, smask, sg, sident,
                bias_row=None, bias_sb=None):
    LD, MM, VE, AC, GP, ST = (sems[k] for k in ("LD", "MM", "VE", "AC", "GP", "ST"))
    for tg in range(B):
        hf3 = src_fm(tg)
        for tc in range(4):
            p.wait("tensor", VE)
            p.wait("tensor", AC)
            if bias_sb is not None:
                p.op("tensor", lambda e: e.matmul(psC[:, 0:8], sones_for_bias[0], bias_sb,
                                                  start=True, stop=False))
            for dc in range(DC):
                p.op("tensor", lambda e, dc=dc, tc=tc, hf3=hf3: e.matmul(
                    psC[:, 0:8], hf3[:, dc, tc * 128:(tc + 1) * 128], srw3[:, dc, :],
                    start=(dc == 0 and bias_sb is None), stop=(dc == DC - 1)),
                    sem=MM if dc == DC - 1 else None)
            p.wait("vector", MM)
            p.op("vector", lambda e: e.tensor_reduce(snm[:, 0:1], psC[:, 0:8], axis=AXX,
                                                     op=OP.max, negate=True), sem=VE)
            p.wait("scalar", VE)
            p.op("scalar", lambda e: e.activation(spe[:], psC[:, 0:8], AF.Exp,
                                                  bias=snm[:, 0:1], scale=1.0,
                                                  accum_out=snm[:, 1:2]), sem=AC)
            p.wait("vector", AC)
            p.op("vector", lambda e: e.tensor_scalar(smask[:], spe[:], 0.99999994, None,
                                                     op0=OP.is_gt), sem=VE)
            p.op("vector", lambda e: e.reciprocal(snm[:, 3:4], snm[:, 1:2]), sem=VE)
            p.wait("vector", VE)
            p.op("vector", lambda e: e.tensor_tensor(smask[:], smask[:], spe[:], op=OP.mult), sem=VE)
            p.wait("vector", VE)
            p.op("vector", lambda e: e.tensor_scalar(smask[:], smask[:], snm[:, 3:4], None,
                                                     op0=OP.mult), sem=VE)
            p.wait("tensor", VE)
            p.op("tensor", lambda e: e.transpose(psD[0:8, 0:128], smask[:], sident[:]), sem=MM)
            p.wait("vector", MM)
            p.op("vector", lambda e, tg=tg, tc=tc: e.tensor_copy(
                sg[:, tg * 512 + tc * 128: tg * 512 + (tc + 1) * 128], psD[0:8, 0:128]), sem=VE)


def build_phase_c(dbg=()):
    nc = bass.Bass("TRN2", target_bir_lowering=False)
    ap = lambda *a, **k: nc.dram_tensor(*a, **k).ap()
    resid_in = ap("resid_in", [B, D, JS], F32, kind="ExternalInput")
    w_inT = ap("w_inT", [D, 3072], F32, kind="ExternalInput")
    b_in = ap("b_in", [1, 3072], F32, kind="ExternalInput")
    w_outT = ap("w_outT", [D, D], F32, kind="ExternalInput")
    b_out_t = ap("b_out_t", [128, DC], F32, kind="ExternalInput")
    consts = ap("consts", [128, len(CONSTS)], F32, kind="ExternalInput")
    onesr = ap("onesr", [1, 128], F32, kind="ExternalInput")
    onesc = ap("onesc", [128, 1], F32, kind="ExternalInput")
    identb = ap("identb", [128, 128], F32, kind="ExternalInput")
    resid3 = ap("resid3", [B, D, JS], F32, kind="ExternalOutput")
    o2scr = ap("o2scr", [D, T], F32, kind="Internal")
    dbgd = dict(dbg)
    dbg_out = {name: ap("dbg_" + name, shape, F32, kind="ExternalOutput")
               for name, shape in dbgd.items()}

    with contextlib.ExitStack() as es:
        E = es.enter_context
        sb = lambda name, shape, dt: E(nc.sbuf_tensor(name, shape, dt))
        ps = lambda name, shape: E(nc.psum_tensor(name, shape, F32))
        xbf = sb("xbf", [128, DC * T], BF16)
        winb = sb("winb", [128, DC * 3072], BF16)
        woutb = sb("woutb", [128, DC * D], BF16)
        qkv = [sb(f"qkv{s}", [128, 3072], BF16) for s in range(B)]
        sbin = sb("sbin", [1, 3072], F32)
        sbout = sb("sbout", [128, DC], F32)
        sconst = sb("sconst", [128, len(CONSTS)], F32)
        sonesr = sb("sonesr", [1, 128], F32)
        sonesc = sb("sonesc", [128, 1], F32)
        sidentb = sb("sidentb", [128, 128], BF16)
        SC = sb("SC", [128, 256], F32)
        SD = sb("SD", [128, 256], F32)
        SE = sb("SE", [128, 256], F32)
        sm4 = sb("sm4", [128, 64], F32)
        ss4 = sb("ss4", [128, 64], F32)
        sprod = sb("sprod", [128, 1024], F32)
        pvacc = sb("pvacc", [128, 1024], F32)
        pvtmp = sb("pvtmp", [128, 1024], F32)
        aobf = sb("aobf", [128, 1024], BF16)
        attnoT = sb("attnoT", [128, DC * 128], BF16)
        o2st = [sb(f"o2st{i}", [128, DC * 128], F32) for i in range(2)]
        rbt = sb("rbt", [128, DC * 512], F32)
        xrt = sb("xrt", [128, DC * 512], F32)
        ssq = sb("ssq", [128, 512], F32)
        srow = sb("srow", [1, 4 * 512], F32)
        psA = ps("psA", [128, 1024])
        psB = ps("psB", [128, 1024])
        psC = ps("psC", [128, 512])
        psD = ps("psD", [128, 512])
        psE = ps("psE", [128, 512])
        psF = ps("psF", [128, 512])
        psEb = psE[:].bitcast(BF16)

        p = Prog(nc)
        LD = Sem(E(nc.semaphore("LD")))
        LG = Sem(E(nc.semaphore("LG")))
        MM = Sem(E(nc.semaphore("MM")))
        VE = Sem(E(nc.semaphore("VE")))
        AC = Sem(E(nc.semaphore("AC")))
        GP = Sem(E(nc.semaphore("GP")))
        ST = Sem(E(nc.semaphore("ST")))
        block = E(nc.Block())

        xbf3 = xbf[:].rearrange("p (dc t) -> p dc t", t=T)
        winb3 = winb[:].rearrange("p (dc f) -> p dc f", f=3072)
        woutb3 = woutb[:].rearrange("p (pc f) -> p pc f", f=D)
        attnoT3 = attnoT[:].rearrange("p (pc t) -> p pc t", t=128)
        o2st3 = [o[:].rearrange("p (fc t) -> p fc t", t=128) for o in o2st]
        rbt3 = rbt[:].rearrange("p (dc t) -> p dc t", t=512)
        xrt3 = xrt[:].rearrange("p (dc t) -> p dc t", t=512)
        rin4 = resid_in.rearrange("b (dc p) j -> p b dc j", p=128)
        rout4 = resid3.rearrange("b (dc p) j -> p b dc j", p=128)
        o2scr3 = o2scr.rearrange("(fc p) t -> p fc t", p=128)
        ROW = lambda i: srow[0:1, i * 512:(i + 1) * 512]

        for bb in range(B):
            p.dma("gpsimd", xbf[:].rearrange("p (dc b j) -> p dc b j", dc=DC, b=B)[:, :, bb, :],
                  rin4[:, bb, :, :], LG)
        p.dma("gpsimd", winb3, w_inT.rearrange("(dc p) f -> p dc f", p=128), LG)
        p.dma("gpsimd", woutb3, w_outT.rearrange("(pc p) f -> p pc f", p=128), LG)
        p.dma("gpsimd", sidentb[:], identb, LG)
        setup_lg = LG.n
        p.dma("sync", sbin[:], b_in, LD)
        p.dma("sync", sbout[:], b_out_t, LD)
        p.dma("sync", sconst[:], consts, LD)
        p.dma("sync", sonesr[:], onesr, LD)
        p.dma("sync", sonesc[:], onesc, LD)
        setup_ld = LD.n

        def fence_v():
            p.wait("vector", VE)

        for jc in range(4):
            for s in range(B):
                tcol = s * JS + jc * 128
                for pg in range(6):
                    bi = pg % 2
                    buf = [psC, psD][bi]
                    p.wait("tensor", LG, setup_lg)
                    p.wait("tensor", LD, setup_ld)
                    p.wait("tensor", AC)
                    for dc in range(DC):
                        p.op("tensor", lambda e, dc=dc, pg=pg, tcol=tcol, buf=buf: e.matmul(
                            buf[:], xbf3[:, dc, tcol:tcol + 128],
                            winb3[:, dc, pg * 512:(pg + 1) * 512],
                            start=(dc == 0), stop=False, skip_group_check=True))
                    p.op("tensor", lambda e, pg=pg, buf=buf: e.matmul(
                        buf[:], sonesr[:], sbin[0:1, pg * 512:(pg + 1) * 512],
                        start=False, stop=True, skip_group_check=True), sem=MM)
                    p.wait("scalar", MM)
                    p.wait("scalar", VE)
                    p.op("scalar", lambda e, s=s, pg=pg, bi=bi: e.copy(
                        qkv[s][:, pg * 512:(pg + 1) * 512], [psC, psD][bi][:]), sem=AC)
            p.wait("vector", AC)
            for t_ in range(B):
                for s in range(B):
                    p.op("vector", lambda e, s=s, t_=t_: e.tensor_tensor(
                        sprod[:], qkv[s][:, 0:1024], qkv[t_][:, 1024:2048], op=OP.mult))
                    p.op("vector", lambda e, s=s, t_=t_: e.tensor_reduce(
                        SC[:, t_ * 64 + s * 16: t_ * 64 + s * 16 + 16],
                        sprod[:].rearrange("p (h d) -> p h d", d=64),
                        axis=AXX, op=OP.add), sem=VE)
            fence_v()
            p.op("vector", lambda e: e.tensor_tensor(sm4[:], SC[:, 0:64], SC[:, 64:128], op=OP.max), sem=VE)
            fence_v()
            p.op("vector", lambda e: e.tensor_tensor(sm4[:], sm4[:], SC[:, 128:192], op=OP.max), sem=VE)
            fence_v()
            p.op("vector", lambda e: e.tensor_tensor(sm4[:], sm4[:], SC[:, 192:256], op=OP.max), sem=VE)
            fence_v()
            p.op("vector", lambda e: e.tensor_tensor(
                SD[:].rearrange("p (t c) -> p t c", t=4), SC[:].rearrange("p (t c) -> p t c", t=4),
                sm4[:].unsqueeze(1).broadcast_to((128, 4, 64)), op=OP.subtract), sem=VE)
            p.wait("scalar", VE)
            p.op("scalar", lambda e: e.activation(SE[:], SD[:], AF.Exp, scale=0.125), sem=AC)
            p.wait("vector", AC)
            p.op("vector", lambda e: e.tensor_tensor(ss4[:], SE[:, 0:64], SE[:, 64:128], op=OP.add), sem=VE)
            fence_v()
            p.op("vector", lambda e: e.tensor_tensor(ss4[:], ss4[:], SE[:, 128:192], op=OP.add), sem=VE)
            fence_v()
            p.op("vector", lambda e: e.tensor_tensor(ss4[:], ss4[:], SE[:, 192:256], op=OP.add), sem=VE)
            fence_v()
            p.op("vector", lambda e: e.reciprocal(sm4[:], ss4[:]), sem=VE)
            fence_v()
            for s in range(B):
                for t_ in range(B):
                    eview = SE[:, t_ * 64 + s * 16: t_ * 64 + s * 16 + 16]
                    ebc = eview.unsqueeze(2).broadcast_to((128, 16, 64))
                    vt = qkv[t_][:, 2048:3072].rearrange("p (h d) -> p h d", d=64)
                    if t_ == 0:
                        p.op("vector", lambda e, ebc=ebc, vt=vt: e.tensor_tensor(
                            pvacc[:].rearrange("p (h d) -> p h d", d=64), vt, ebc, op=OP.mult))
                    else:
                        p.op("vector", lambda e, ebc=ebc, vt=vt: e.tensor_tensor(
                            pvtmp[:].rearrange("p (h d) -> p h d", d=64), vt, ebc, op=OP.mult))
                        p.op("vector", lambda e: e.tensor_tensor(pvacc[:], pvacc[:], pvtmp[:],
                                                                 op=OP.add))
                rbc = sm4[:, s * 16:(s + 1) * 16].unsqueeze(2).broadcast_to((128, 16, 64))
                p.op("vector", lambda e, rbc=rbc: e.tensor_tensor(
                    aobf[:].rearrange("p (h d) -> p h d", d=64),
                    pvacc[:].rearrange("p (h d) -> p h d", d=64), rbc, op=OP.mult), sem=VE)
                for pc in range(DC):
                    p.wait("tensor", VE)
                    p.op("tensor", lambda e, pc=pc: e.transpose(
                        psEb[:, 0:128], aobf[:, pc * 128:(pc + 1) * 128], sidentb[:]), sem=MM)
                    p.wait("vector", MM)
                    p.op("vector", lambda e, pc=pc: e.tensor_copy(
                        attnoT3[:, pc, :], psEb[:, 0:128]), sem=VE)
                tcol = s * JS + jc * 128
                oi = (jc * B + s) % 2
                for fc in range(DC):
                    bi = fc % 2
                    buf = [psC, psD][bi]
                    p.wait("tensor", VE)
                    p.wait("tensor", ST)
                    for pc in range(DC):
                        p.op("tensor", lambda e, pc=pc, fc=fc, buf=buf: e.matmul(
                            buf[:, 0:128], woutb3[:, pc, fc * 128:(fc + 1) * 128],
                            attnoT3[:, pc, :],
                            start=(pc == 0), stop=(pc == DC - 1)),
                            sem=MM if pc == DC - 1 else None)
                    p.wait("vector", MM)
                    p.op("vector", lambda e, fc=fc, bi=bi, oi=oi: e.tensor_scalar(
                        o2st3[oi][:, fc, :], [psC, psD][bi][:, 0:128],
                        sbout[:, fc:fc + 1], None, op0=OP.add), sem=VE)
                p.wait("sync", VE)
                p.dma("sync", o2scr3[:, :, tcol:tcol + 128], o2st3[oi], ST)

        for tg in range(B):
            p.wait("sync", VE)
            p.wait("sync", GP)
            p.wait("sync", ST)
            ld0 = p.dma("sync", xrt3, rin4[:, tg, :, :], LD)
            ld1 = p.dma("sync", rbt3, o2scr3[:, :, tg * 512:(tg + 1) * 512], LD)
            p.wait("vector", LD, ld1)
            p.wait("vector", ST)
            for fc in range(DC):
                p.op("vector", lambda e, fc=fc: e.tensor_tensor(
                    rbt3[:, fc, :], rbt3[:, fc, :], xrt3[:, fc, :], op=OP.add), sem=VE)
            sq_read = 0
            p.wait("tensor", VE)
            for fc in range(DC):
                p.wait("scalar", VE)
                p.wait("scalar", MM, sq_read)
                p.op("scalar", lambda e, fc=fc: e.activation(
                    ssq[:], rbt3[:, fc, :], AF.Square, scale=1.0), sem=AC)
                p.op("tensor", lambda e, fc=fc: e.matmul(
                    psE[0:1, :], sonesc[:], rbt3[:, fc, :],
                    start=(fc == 0), stop=(fc == DC - 1), skip_group_check=True),
                    sem=MM if fc == DC - 1 else None)
                p.wait("tensor", AC)
                p.op("tensor", lambda e, fc=fc: e.matmul(
                    psF[0:1, :], sonesc[:], ssq[:],
                    start=(fc == 0), stop=(fc == DC - 1), skip_group_check=True), sem=MM)
                sq_read = MM.n
            p.wait("vector", MM)
            p.op("vector", lambda e: e.tensor_scalar(ROW(0), psE[0:1, :], 1.0 / D, None, op0=OP.mult))
            p.op("vector", lambda e: e.tensor_scalar(ROW(1), psF[0:1, :], 1.0 / D, None, op0=OP.mult))
            p.op("vector", lambda e: e.tensor_tensor(ROW(2), ROW(0), ROW(0), op=OP.mult))
            p.op("vector", lambda e: e.tensor_tensor(ROW(2), ROW(1), ROW(2), op=OP.subtract), sem=VE)
            p.wait("scalar", VE)
            p.op("scalar", lambda e: e.activation(ROW(3), ROW(2), AF.Ln,
                                                  bias=sconst[0:1, IC_EPS:IC_EPS + 1], scale=1.0))
            p.op("scalar", lambda e: e.activation(ROW(2), ROW(3), AF.Exp, scale=-0.5), sem=AC)
            p.wait("vector", AC)
            p.op("vector", lambda e: e.tensor_tensor(ROW(3), ROW(0), ROW(2), op=OP.mult), sem=VE)
            p.wait("vector", VE)
            p.op("vector", lambda e: e.tensor_scalar(ROW(3), ROW(3), -1.0, None, op0=OP.mult), sem=VE)
            p.wait("tensor", VE)
            p.op("tensor", lambda e: e.matmul(psA[:, 0:512], sonesr[:], ROW(2), start=True, stop=True))
            p.op("tensor", lambda e: e.matmul(psA[:, 512:1024], sonesr[:], ROW(3), start=True, stop=True),
                 sem=MM)
            p.wait("vector", MM)
            for fc in range(DC):
                p.op("vector", lambda e, fc=fc: e.tensor_tensor(
                    ssq[:], rbt3[:, fc, :], psA[:, 0:512], op=OP.mult))
                p.op("vector", lambda e, fc=fc: e.tensor_tensor(
                    rbt3[:, fc, :], ssq[:], psA[:, 512:1024], op=OP.add), sem=VE)
            p.wait("sync", VE)
            p.dma("sync", rout4[:, tg, :, :], rbt3, ST)

        dbg_src = {"qkv0": qkv[0], "SC": SC, "SE": SE, "attnoT": attnoT}
        for name in dbgd:
            p.wait("gpsimd", VE)
            p.wait("gpsimd", MM)
            p.wait("gpsimd", GP)
            p.dma("gpsimd", dbg_out[name], dbg_src[name][:], ST)
        p.wait("sync", ST)
        p.wait("gpsimd", ST)
        p.install(block)
    return nc


def build_phase_d(dbg=()):
    NE = 8
    nc = bass.Bass("TRN2", target_bir_lowering=False)
    ap = lambda *a, **k: nc.dram_tensor(*a, **k).ap()
    resid_in = ap("resid_in", [B, D, JS], F32, kind="ExternalInput")
    w1 = ap("w1", [NE, D, 4096], F32, kind="ExternalInput")
    b1t = ap("b1t", [128, NE * 32], F32, kind="ExternalInput")
    w2 = ap("w2", [NE, 4096, D], F32, kind="ExternalInput")
    b2t = ap("b2t", [128, NE * DC], F32, kind="ExternalInput")
    routerW = ap("routerW", [D, 8], F32, kind="ExternalInput")
    routerb = ap("routerb", [1, 8], F32, kind="ExternalInput")
    consts = ap("consts", [128, len(CONSTS)], F32, kind="ExternalInput")
    onesr = ap("onesr", [1, 128], F32, kind="ExternalInput")
    onesc = ap("onesc", [128, 1], F32, kind="ExternalInput")
    ident = ap("ident", [128, 128], F32, kind="ExternalInput")
    outx = ap("outx", [B, JS, D], F32, kind="ExternalOutput")
    gscr = ap("gscr", [8, T], F32, kind="Internal")
    w1scr = ap("w1scr", [NE, D, 4096], BF16, kind="Internal")
    dbgd = dict(dbg)
    dbg_out = {name: ap("dbg_" + name, shape, F32, kind="ExternalOutput")
               for name, shape in dbgd.items()}

    with contextlib.ExitStack() as es:
        E = es.enter_context
        sb = lambda name, shape, dt: E(nc.sbuf_tensor(name, shape, dt))
        ps = lambda name, shape: E(nc.psum_tensor(name, shape, F32))
        rbig = sb("rbig", [128, DC * T], F32)
        xbf = sb("xbf", [128, DC * T], BF16)
        w2res = sb("w2res", [128, 32 * D], BF16)
        h1q = sb("h1q", [128, 32 * 256], BF16)
        w1st = [sb(f"w1st{i}", [128, DC * 128], BF16) for i in range(2)]
        w1cast = sb("w1cast", [128, 4096], BF16)
        sb1 = sb("sb1", [128, NE * 32], F32)
        sb2 = sb("sb2", [128, NE * DC], F32)
        srw = sb("srw", [128, DC * 8], F32)
        srb = sb("srb", [1, 8], F32)
        sg = sb("sg", [8, T], F32)
        sconst = sb("sconst", [128, len(CONSTS)], F32)
        sonesr = sb("sonesr", [1, 128], F32)
        sonesc = sb("sonesc", [128, 1], F32)
        sident = sb("sident", [128, 128], F32)
        spe = sb("spe", [128, 8], F32)
        snm = sb("snm", [128, 4], F32)
        smask = sb("smask", [128, 8], F32)
        ssq = sb("ssq", [128, 512], F32)
        ostg = sb("ostg", [128, 1024], F32)
        ytmp = [ostg[:, 0:256], ostg[:, 256:512]]
        srow = h1q[:].bitcast(F32)[0:1, 0:2048]
        grow = [sg[0:1, i * 256:(i + 1) * 256] for i in range(2)]
        psA = ps("psA", [128, 1024])
        psB = ps("psB", [128, 1024])
        psC = ps("psC", [128, 512])
        psD = ps("psD", [128, 512])
        psE = ps("psE", [128, 512])
        psF = ps("psF", [128, 512])

        p = Prog(nc)
        LD = Sem(E(nc.semaphore("LD")))
        LG = Sem(E(nc.semaphore("LG")))
        MM = Sem(E(nc.semaphore("MM")))
        VE = Sem(E(nc.semaphore("VE")))
        AC = Sem(E(nc.semaphore("AC")))
        GP = Sem(E(nc.semaphore("GP")))
        ST = Sem(E(nc.semaphore("ST")))
        block = E(nc.Block())

        rbig3 = rbig[:].rearrange("p (dc t) -> p dc t", t=T)
        xbf3 = xbf[:].rearrange("p (dc t) -> p dc t", t=T)
        w2res3 = w2res[:].rearrange("p (fc d) -> p fc d", d=D)
        h1q3 = h1q[:].rearrange("p (fc t) -> p fc t", t=256)
        w1st3 = [w[:].rearrange("p (dc f) -> p dc f", f=128) for w in w1st]
        srw3 = srw[:].rearrange("p (dc e) -> p dc e", e=8)
        rin4 = resid_in.rearrange("b (dc p) j -> p b dc j", p=128)
        out3 = outx.rearrange("b j d -> b j d")
        w1r = w1.rearrange("e (dc p) f -> p e dc f", p=128)
        w1sr = w1scr.rearrange("e (dc p) f -> p e dc f", p=128)
        w2r = w2.rearrange("e (fc p) d -> p e fc d", p=128)
        ROW = lambda i: srow[0:1, i * 512:(i + 1) * 512]

        for bb in range(B):
            p.dma("sync", rbig[:].rearrange("p (dc b j) -> p dc b j", dc=DC, b=B)[:, :, bb, :],
                  rin4[:, bb, :, :], LD)
            p.dma("gpsimd", xbf[:].rearrange("p (dc b j) -> p dc b j", dc=DC, b=B)[:, :, bb, :],
                  rin4[:, bb, :, :], LG)
        p.dma("sync", sb1[:], b1t, LD)
        p.dma("sync", sb2[:], b2t, LD)
        p.dma("sync", srw3, routerW.rearrange("(dc p) e -> p dc e", p=128), LD)
        p.dma("sync", srb[:], routerb, LD)
        p.dma("sync", sconst[:], consts, LD)
        p.dma("sync", sonesr[:], onesr, LD)
        p.dma("sync", sonesc[:], onesc, LD)
        p.dma("sync", sident[:], ident, LD)
        setup_ld = LD.n

        for e in range(NE):
            for dc in range(DC):
                p.wait("gpsimd", ST)
                tok = p.dma("gpsimd", w1cast[:], w1r[:, e, dc, :], LG)
                p.wait("sync", LG, tok)
                p.dma("sync", w1sr[:, e, dc, :], w1cast[:], ST)
        w1cast_done = ST.n

        p.wait("tensor", LD, setup_ld)
        for tg in range(B):
            for tc in range(4):
                tcol = tg * 512 + tc * 128
                p.wait("tensor", VE)
                p.wait("tensor", AC)
                p.op("tensor", lambda e: e.matmul(psC[:, 0:8], sonesr[:], srb[:],
                                                  start=True, stop=False, skip_group_check=True))
                for dc in range(DC):
                    p.op("tensor", lambda e, dc=dc, tcol=tcol: e.matmul(
                        psC[:, 0:8], rbig3[:, dc, tcol:tcol + 128], srw3[:, dc, :],
                        start=False, stop=(dc == DC - 1), skip_group_check=True),
                        sem=MM if dc == DC - 1 else None)
                p.wait("vector", MM)
                p.op("vector", lambda e: e.tensor_reduce(snm[:, 0:1], psC[:, 0:8], axis=AXX,
                                                         op=OP.max, negate=True), sem=VE)
                p.wait("scalar", VE)
                p.op("scalar", lambda e: e.activation(spe[:], psC[:, 0:8], AF.Exp,
                                                      bias=snm[:, 0:1], scale=1.0,
                                                      accum_out=snm[:, 1:2]), sem=AC)
                p.wait("vector", AC)
                p.op("vector", lambda e: e.tensor_scalar(smask[:], spe[:], 0.99999994, None,
                                                         op0=OP.is_gt), sem=VE)
                p.op("vector", lambda e: e.reciprocal(snm[:, 3:4], snm[:, 1:2]), sem=VE)
                p.wait("vector", VE)
                p.op("vector", lambda e: e.tensor_tensor(smask[:], smask[:], spe[:], op=OP.mult), sem=VE)
                p.wait("vector", VE)
                p.op("vector", lambda e: e.tensor_scalar(smask[:], smask[:], snm[:, 3:4], None,
                                                         op0=OP.mult), sem=VE)
                p.wait("tensor", VE)
                p.op("tensor", lambda e: e.transpose(psD[0:8, 0:128], smask[:], sident[:]), sem=MM)
                p.wait("vector", MM)
                p.op("vector", lambda e, tcol=tcol: e.tensor_copy(
                    sg[:, tcol:tcol + 128], psD[0:8, 0:128]), sem=VE)
        p.wait("sync", VE)
        gscr_tok = p.dma("sync", gscr, sg[:], ST)

        for e in range(NE):
            for fc in range(32):
                p.wait("gpsimd", MM)
                p.dma("gpsimd", w2res3[:, fc, :], w2r[:, e, fc, :], LG)
            w2_tok = LG.n
            for tq in range(8):
                for fc in range(32):
                    bi = fc % 2
                    buf = [psC, psD][bi]
                    wi = fc % 2
                    p.wait("gpsimd", MM)
                    p.wait("gpsimd", ST, w1cast_done)
                    w1_tok = p.dma("gpsimd", w1st3[wi],
                                   w1sr[:, e, :, fc * 128:(fc + 1) * 128], LG)
                    p.wait("tensor", LG, w1_tok)
                    p.wait("tensor", AC)
                    for dc in range(DC):
                        p.op("tensor", lambda e_, dc=dc, fc=fc, tq=tq, buf=buf, wi=wi: e_.matmul(
                            buf[:, 0:256], w1st3[wi][:, dc, :], xbf3[:, dc, tq * 256:(tq + 1) * 256],
                            start=(dc == 0), stop=(dc == DC - 1)),
                            sem=MM if dc == DC - 1 else None)
                    p.wait("scalar", MM)
                    p.op("scalar", lambda e_, fc=fc, bi=bi, e=e: e_.activation(
                        h1q3[:, fc, :], [psC, psD][bi][:, 0:256], AF.Relu,
                        bias=sb1[:, e * 32 + fc: e * 32 + fc + 1], scale=1.0), sem=AC)
                for dc8 in range(DC):
                    bi = dc8 % 2
                    buf = [psE, psF][bi]
                    p.wait("tensor", AC)
                    p.wait("tensor", VE)
                    for fc in range(32):
                        p.op("tensor", lambda e_, fc=fc, dc8=dc8, buf=buf: e_.matmul(
                            buf[:, 0:256], w2res3[:, fc, dc8 * 128:(dc8 + 1) * 128], h1q3[:, fc, :],
                            start=(fc == 0), stop=(fc == 31)),
                            sem=MM if fc == 31 else None)
                    gi = dc8 % 2
                    p.wait("sync", ST, gscr_tok)
                    p.wait("sync", MM)
                    gtok = p.dma("sync", grow[gi], gscr[e:e + 1, tq * 256:(tq + 1) * 256], LD)
                    p.wait("tensor", LD, gtok)
                    p.op("tensor", lambda e_, gi=gi: e_.matmul(
                        psA[:, 0:256], sonesr[:], grow[gi],
                        start=True, stop=True), sem=MM)
                    p.wait("vector", MM)
                    p.wait("vector", GP)
                    p.op("vector", lambda e_, bi=bi, dc8=dc8, e=e, buf=buf: e_.tensor_scalar(
                        ytmp[bi], buf[:, 0:256], sb2[:, e * DC + dc8: e * DC + dc8 + 1], None,
                        op0=OP.add), sem=VE)
                    p.op("vector", lambda e_, bi=bi: e_.tensor_tensor(
                        ytmp[bi], ytmp[bi], psA[:, 0:256], op=OP.mult), sem=VE)
                    p.wait("gpsimd", VE)
                    p.op("gpsimd", lambda e_, bi=bi, dc8=dc8, tq=tq: e_.tensor_tensor(
                        rbig3[:, dc8, tq * 256:(tq + 1) * 256],
                        rbig3[:, dc8, tq * 256:(tq + 1) * 256], ytmp[bi], op=OP.add), sem=GP)

        for tg in range(B):
            tcol = tg * 512
            sq_read = 0
            p.wait("tensor", GP)
            for fc in range(DC):
                p.wait("scalar", GP)
                p.wait("scalar", VE)
                p.wait("scalar", MM, sq_read)
                p.op("scalar", lambda e, fc=fc, tcol=tcol: e.activation(
                    ssq[:], rbig3[:, fc, tcol:tcol + 512], AF.Square, scale=1.0), sem=AC)
                p.op("tensor", lambda e, fc=fc, tcol=tcol: e.matmul(
                    psE[0:1, :], sonesc[:], rbig3[:, fc, tcol:tcol + 512],
                    start=(fc == 0), stop=(fc == DC - 1), skip_group_check=True),
                    sem=MM if fc == DC - 1 else None)
                p.wait("tensor", AC)
                p.op("tensor", lambda e, fc=fc: e.matmul(
                    psF[0:1, :], sonesc[:], ssq[:],
                    start=(fc == 0), stop=(fc == DC - 1), skip_group_check=True), sem=MM)
                sq_read = MM.n
            p.wait("vector", MM)
            p.op("vector", lambda e: e.tensor_scalar(ROW(0), psE[0:1, :], 1.0 / D, None, op0=OP.mult))
            p.op("vector", lambda e: e.tensor_scalar(ROW(1), psF[0:1, :], 1.0 / D, None, op0=OP.mult))
            p.op("vector", lambda e: e.tensor_tensor(ROW(2), ROW(0), ROW(0), op=OP.mult))
            p.op("vector", lambda e: e.tensor_tensor(ROW(2), ROW(1), ROW(2), op=OP.subtract), sem=VE)
            p.wait("scalar", VE)
            p.op("scalar", lambda e: e.activation(ROW(3), ROW(2), AF.Ln,
                                                  bias=sconst[0:1, IC_EPS:IC_EPS + 1], scale=1.0))
            p.op("scalar", lambda e: e.activation(ROW(2), ROW(3), AF.Exp, scale=-0.5), sem=AC)
            p.wait("vector", AC)
            p.op("vector", lambda e: e.tensor_tensor(ROW(3), ROW(0), ROW(2), op=OP.mult), sem=VE)
            p.wait("vector", VE)
            p.op("vector", lambda e: e.tensor_scalar(ROW(3), ROW(3), -1.0, None, op0=OP.mult), sem=VE)
            p.wait("tensor", VE)
            p.op("tensor", lambda e: e.matmul(psA[:, 0:512], sonesr[:], ROW(2), start=True, stop=True))
            p.op("tensor", lambda e: e.matmul(psA[:, 512:1024], sonesr[:], ROW(3), start=True, stop=True),
                 sem=MM)
            p.wait("vector", MM)
            for fc in range(DC):
                p.op("vector", lambda e, fc=fc, tcol=tcol: e.tensor_tensor(
                    ssq[:], rbig3[:, fc, tcol:tcol + 512], psA[:, 0:512], op=OP.mult))
                p.op("vector", lambda e, fc=fc, tcol=tcol: e.tensor_tensor(
                    rbig3[:, fc, tcol:tcol + 512], ssq[:], psA[:, 512:1024], op=OP.add), sem=VE)
            for jb in range(4):
                for dcc in range(DC):
                    p.wait("tensor", VE)
                    p.wait("tensor", AC)
                    p.op("tensor", lambda e, dcc=dcc, tcol=tcol, jb=jb: e.transpose(
                        psB[:, 0:128], rbig3[:, dcc, tcol + jb * 128: tcol + (jb + 1) * 128],
                        sident[:]), sem=MM)
                    p.wait("scalar", MM)
                    p.wait("scalar", ST)
                    p.op("scalar", lambda e, dcc=dcc: e.copy(
                        ostg[:, dcc * 128:(dcc + 1) * 128], psB[:, 0:128]), sem=AC)
                p.wait("sync", AC)
                p.dma("sync", out3[tg, jb * 128:(jb + 1) * 128, :], ostg[:], ST)

        dbg_src = {"g": sg, "rmid": rbig, "h1": h1q}
        for name in dbgd:
            p.wait("gpsimd", VE)
            p.wait("gpsimd", MM)
            p.wait("gpsimd", GP)
            if name == "gates":
                p.dma("gpsimd", dbg_out[name], gscr, ST)
            else:
                p.dma("gpsimd", dbg_out[name], dbg_src[name][:], ST)
        p.wait("sync", ST)
        p.wait("gpsimd", ST)
        p.install(block)
    return nc



def rope_tables():
    d = DH
    inv = (1.0 / (10000.0 ** (np.arange(0, d, 2, dtype=np.float32) / np.float32(d)))).astype(np.float32)
    pos = np.arange(N, dtype=np.float32)
    fr = np.outer(pos, inv).astype(np.float32)
    fr = np.concatenate([fr, fr], axis=1)
    cos = np.cos(fr).astype(np.float32)
    sin = np.sin(fr).astype(np.float32)
    sgn = np.ones(64, np.float32)
    sgn[:32] = -1.0
    ss = sin * sgn
    ssw = np.concatenate([ss[:, 32:], ss[:, :32]], axis=1)
    cos_fm = np.ascontiguousarray(cos.T)
    ssw_fm = np.ascontiguousarray(ssw.T)
    return (np.ascontiguousarray(np.concatenate([cos_fm, cos_fm], axis=0)),
            np.ascontiguousarray(np.concatenate([ssw_fm, ssw_fm], axis=0)))


def perm_wq_cols():
    idx = np.zeros(D, np.int64)
    for qc in range(DC):
        for half in range(2):
            h = head_at(qc, half)
            idx[qc * 128 + half * 64: qc * 128 + half * 64 + 64] = np.arange(h * DH, (h + 1) * DH)
    return idx


def phase_a_inputs(L0_attn_x, L0_attn_Wq, L0_attn_Wkv, L0_attn_Wo, L0_attn_bo):
    x = np.asarray(L0_attn_x)
    x_fm = np.ascontiguousarray(x.transpose(0, 2, 1))
    cosk, sinksw = rope_tables()
    idx = perm_wq_cols()
    wq_p = np.ascontiguousarray(np.asarray(L0_attn_Wq)[:, idx])
    wo_p = np.ascontiguousarray(np.asarray(L0_attn_Wo)[idx, :])
    wkv = np.ascontiguousarray(np.asarray(L0_attn_Wkv))
    bo_t = np.ascontiguousarray(np.asarray(L0_attn_bo).reshape(DC, 128).T)
    consts = np.tile(np.array(CONSTS, np.float32)[None, :], (128, 1))
    onesr = np.ones((1, 128), np.float32)
    onesc = np.ones((128, 1), np.float32)
    maps = []
    for c in range(CORES):
        j0 = c * JS
        maps.append(dict(
            x_fm=x_fm,
            x_own=np.ascontiguousarray(x_fm[:, :, j0:j0 + JS]),
            wq=wq_p, wkv=wkv, wo=wo_p, bo_t=bo_t,
            cosk=cosk, sinksw=sinksw,
            cosq=np.ascontiguousarray(cosk[:, j0:j0 + JS]),
            sinqsw=np.ascontiguousarray(sinksw[:, j0:j0 + JS]),
            consts=consts, onesr=onesr, onesc=onesc,
        ))
    return maps


def phase_b_inputs(inputs):
    A = lambda k: np.asarray(inputs[k], np.float32)
    ew1, ew2 = A("L0_ffn_expert_w1"), A("L0_ffn_expert_w2")
    sw1, sw2 = A("L0_ffn_shared_w1"), A("L0_ffn_shared_w2")
    w1nat = np.concatenate([ew1, sw1[None]], axis=0)
    w1T = np.ascontiguousarray(w1nat.transpose(0, 2, 1))
    w2nat = np.concatenate([ew2, sw2[None]], axis=0)
    w2T = np.ascontiguousarray(w2nat.transpose(0, 2, 1))
    routerT = np.ascontiguousarray(A("L0_ffn_router_w").T)
    gamma_t = np.ascontiguousarray(A("L0_ffn_ln_gamma").reshape(DC, 128).T)
    beta_t = np.ascontiguousarray(A("L0_ffn_ln_beta").reshape(DC, 128).T)
    consts = np.tile(np.array(CONSTS, np.float32)[None, :], (128, 1))
    com = dict(w1T=w1T, w1nat=np.ascontiguousarray(w1nat), w2T=w2T,
               w2nat=np.ascontiguousarray(w2nat), routerT=routerT,
               gamma_t=gamma_t, beta_t=beta_t, consts=consts,
               onesr=np.ones((1, 128), np.float32),
               onesc=np.ones((128, 1), np.float32),
               negr=np.full((1, 128), -1.0, np.float32),
               ident=np.eye(128, dtype=np.float32))
    return com


def phase_c_inputs(inputs):
    A = lambda k: np.asarray(inputs[k], np.float32)
    w_inT = np.ascontiguousarray(A("L1_attn_in_proj_weight").T)
    b_in = np.ascontiguousarray(A("L1_attn_in_proj_bias")[None, :])
    w_outT = np.ascontiguousarray(A("L1_attn_out_proj_weight").T)
    b_out_t = np.ascontiguousarray(A("L1_attn_out_proj_bias").reshape(DC, 128).T)
    consts = np.tile(np.array(CONSTS, np.float32)[None, :], (128, 1))
    return dict(w_inT=w_inT, b_in=b_in, w_outT=w_outT, b_out_t=b_out_t,
                consts=consts, onesr=np.ones((1, 128), np.float32),
                onesc=np.ones((128, 1), np.float32),
                identb=np.eye(128, dtype=np.float32))


def phase_d_inputs(inputs):
    A = lambda k: np.asarray(inputs[k], np.float32)
    w1 = np.ascontiguousarray(A("L1_ffn_W1"))
    b1t = np.ascontiguousarray(A("L1_ffn_b1").reshape(8, 32, 128)
                               .transpose(2, 0, 1).reshape(128, 256))
    w2 = np.ascontiguousarray(A("L1_ffn_W2"))
    b2t = np.ascontiguousarray(A("L1_ffn_b2").reshape(8, DC, 128)
                               .transpose(2, 0, 1).reshape(128, 8 * DC))
    routerW = np.ascontiguousarray(A("L1_ffn_router_W"))
    routerb = np.ascontiguousarray(A("L1_ffn_router_b")[None, :])
    consts = np.tile(np.array(CONSTS, np.float32)[None, :], (128, 1))
    return dict(w1=w1, b1t=b1t, w2=w2, b2t=b2t, routerW=routerW, routerb=routerb,
                consts=consts, onesr=np.ones((1, 128), np.float32),
                onesc=np.ones((128, 1), np.float32),
                ident=np.eye(128, dtype=np.float32))


def kernel(**inputs):
    maps_a = phase_a_inputs(inputs["L0_attn_x"], inputs["L0_attn_Wq"],
                            inputs["L0_attn_Wkv"], inputs["L0_attn_Wo"], inputs["L0_attn_bo"])
    nc_a = build_phase_a()
    res_a = run_bass_kernel_spmd(nc_a, maps_a, core_ids=list(range(CORES)))
    r1 = [r["resid1"] for r in res_a.results]

    com_b = phase_b_inputs(inputs)
    maps_b = [dict(com_b, resid_in=r1[c]) for c in range(CORES)]
    nc_b = build_phase_b()
    res_b = run_bass_kernel_spmd(nc_b, maps_b, core_ids=list(range(CORES)))
    r2 = [r["resid2"] for r in res_b.results]

    com_c = phase_c_inputs(inputs)
    maps_c = [dict(com_c, resid_in=r2[c]) for c in range(CORES)]
    nc_c = build_phase_c()
    res_c = run_bass_kernel_spmd(nc_c, maps_c, core_ids=list(range(CORES)))
    r3 = [r["resid3"] for r in res_c.results]

    com_d = phase_d_inputs(inputs)
    maps_d = [dict(com_d, resid_in=r3[c]) for c in range(CORES)]
    nc_d = build_phase_d()
    res_d = run_bass_kernel_spmd(nc_d, maps_d, core_ids=list(range(CORES)))
    outs = [r["outx"] for r in res_d.results]

    full = np.concatenate(outs, axis=1)
    return np.ascontiguousarray(full).astype(np.float32)
